# version 24
# baseline (speedup 1.0000x reference)
from contextlib import ExitStack

import numpy as np
import ml_dtypes
import concourse.bass as bass
import concourse.mybir as mybir
from concourse.bass_utils import run_bass_kernel_spmd

F32 = mybir.dt.float32
BF16 = mybir.dt.bfloat16
AF = mybir.ActivationFunctionType
ALU = mybir.AluOpType
AX = mybir.AxisListType

T = 2048
D = 1024
H = 16
HD = 64
DFF = 4096
RK = 256
NCC = D // 128
NTT = T // 128
SC = 0.125
EPS = 1e-5
ACONST = 1.0 + 2048e-8
BCONST = 2048e-14
MASKVAL = -1.0e30


class Prog:

    ENG = ("tensor", "vector", "scalar", "gpsimd", "sync")

    def __init__(self, nc):
        self.nc = nc
        self.eng_ops = {e: [] for e in self.ENG}
        self.eng_seq = {e: 0 for e in self.ENG}
        self.last_w = {}
        self.readers = {}
        self.clock = {e: {} for e in self.ENG}
        self.snap = {}
        self.streams = {}
        self.marked = set()
        self._bar = 0

    def _join(self, eng, ev, ck, v):
        clk = self.clock[eng]
        for k2, v2 in self.snap[ev].items():
            if clk.get(k2, -1) < v2:
                clk[k2] = v2
        if clk.get(ck, -1) < v:
            clk[ck] = v

    def _need(self, eng, ev, waits):
        if ev is None:
            return
        if ev[0] == "E":
            _, e2, s2 = ev
            if e2 == "tensor" and eng == "tensor":
                return
            ck, v = ("E", e2), s2
        else:
            _, st, k = ev
            ck, v = ("D", st), k
        if self.clock[eng].get(ck, -1) >= v:
            return
        waits.append(ev)
        if ev[0] == "E":
            self.marked.add(ev)
        self._join(eng, ev, ck, v)

    def _deps(self, eng, reads, writes, extra):
        waits = []
        for r in reads:
            self._need(eng, self.last_w.get(r), waits)
        for w in writes:
            self._need(eng, self.last_w.get(w), waits)
            for ev in self.readers.get(w, ()):
                self._need(eng, ev, waits)
        for ev in extra:
            self._need(eng, ev, waits)
        return waits

    def op(self, eng, fn, reads=(), writes=(), extra=()):
        waits = self._deps(eng, reads, writes, extra)
        seq = self.eng_seq[eng]
        self.eng_seq[eng] = seq + 1
        me = ("E", eng, seq)
        sn = dict(self.clock[eng])
        sn[("E", eng)] = seq
        self.snap[me] = sn
        for r in reads:
            self.readers.setdefault(r, []).append(me)
        for w in writes:
            self.last_w[w] = me
            self.readers[w] = []
        self.eng_ops[eng].append((fn, waits, me, None))
        return me

    def dma(self, stream, qeng, fn, reads=(), writes=()):
        k = self.streams.get(stream, 0) + 1
        self.streams[stream] = k
        extra = [("D", stream, k - 1)] if k >= 2 else []
        waits = self._deps(qeng, reads, writes, extra)
        ev = ("D", stream, k)
        sn = dict(self.clock[qeng])
        sn[("D", stream)] = k
        self.snap[ev] = sn
        for r in reads:
            self.readers.setdefault(r, []).append(ev)
        for w in writes:
            self.last_w[w] = ev
            self.readers[w] = []
        seq = self.eng_seq[qeng]
        self.eng_seq[qeng] = seq + 1
        me = ("E", qeng, seq)
        sn2 = dict(sn)
        sn2[("E", qeng)] = seq
        self.snap[me] = sn2
        self.eng_ops[qeng].append((fn, waits, me, stream))
        return ev

    def barrier(self):
        self._bar += 1
        b = self._bar
        for e in self.ENG:
            if e == "sync":
                continue
            self.op(e, lambda eng: eng.drain(), writes=((f"__bi{b}", e),))
        extra = [("D", s, k) for s, k in self.streams.items()]
        self.op("sync", lambda eng: eng.nop(),
                reads=tuple((f"__bi{b}", e) for e in self.ENG if e != "sync"),
                writes=((f"__bo{b}",),), extra=tuple(extra))
        for e in self.ENG:
            if e == "sync":
                continue
            self.op(e, lambda eng: eng.nop(), reads=((f"__bo{b}",),))

    def finalize(self, stack):
        nc = self.nc
        psem = {e: stack.enter_context(nc.semaphore(f"p_{e}"))
                for e in self.ENG if any(m[1] == e for m in self.marked)}
        ssem = {s: stack.enter_context(nc.semaphore(f"s_{s}"))
                for s in self.streams}
        ccount = {}
        for e in self.ENG:
            cnt = 0
            arr = []
            for (_, _, me, dmastream) in self.eng_ops[e]:
                if dmastream is None and me in self.marked:
                    cnt += 1
                arr.append(cnt)
            ccount[e] = arr

        def replay(ename, engobj):
            for (fn, waits, me, dmastream) in self.eng_ops[ename]:
                for ev in waits:
                    if ev[0] == "E":
                        engobj.wait_ge(psem[ev[1]], ccount[ev[1]][ev[2]])
                    else:
                        engobj.wait_ge(ssem[ev[1]], 16 * ev[2])
                inst = fn(engobj)
                if dmastream is not None:
                    inst.then_inc(ssem[dmastream], 16)
                elif me in self.marked:
                    inst.then_inc(psem[me[1]], 1)

        block = stack.enter_context(nc.Block())

        @block.tensor
        def _(eng):
            replay("tensor", eng)

        @block.vector
        def _(eng):
            replay("vector", eng)

        @block.scalar
        def _(eng):
            replay("scalar", eng)

        @block.gpsimd
        def _(eng):
            replay("gpsimd", eng)

        @block.sync
        def _(eng):
            replay("sync", eng)


def build_nc():
    nc = bass.Bass("TRN2", target_bir_lowering=False, debug=False)
    i_x = nc.dram_tensor("xT", [D, T], F32, kind="ExternalInput").ap()
    i_wqk = nc.dram_tensor("wqkT", [D, 2 * D], F32, kind="ExternalInput").ap()
    i_wv = nc.dram_tensor("wvT", [D, D], F32, kind="ExternalInput").ap()
    i_wp = nc.dram_tensor("wpT", [D, D], F32, kind="ExternalInput").ap()
    i_u1 = nc.dram_tensor("U1", [D, RK], F32, kind="ExternalInput").ap()
    i_v1 = nc.dram_tensor("V1", [RK, DFF], F32, kind="ExternalInput").ap()
    i_u2 = nc.dram_tensor("U2", [DFF, RK], F32, kind="ExternalInput").ap()
    i_v2 = nc.dram_tensor("V2", [RK, D], F32, kind="ExternalInput").ap()
    i_b1 = nc.dram_tensor("b1h", [128, 32], F32, kind="ExternalInput").ap()
    i_b2 = nc.dram_tensor("b2h", [128, 8], F32, kind="ExternalInput").ap()
    o_out = nc.dram_tensor("out", [D, T], F32, kind="ExternalOutput").ap()

    p = Prog(nc)
    stack = ExitStack()

    def alloc(st, name, shape, dt, side="left"):
        return st.enter_context(nc.sbuf_tensor(name, shape, dt, side=side))

    ones_col = alloc(stack, "ones_col", [128, 1], BF16)
    ones_row = alloc(stack, "ones_row", [1, 128], BF16)
    ones_rowf = alloc(stack, "ones_rowf", [1, 128], F32)
    b1_sb = alloc(stack, "b1_sb", [128, 32], F32)
    cst = alloc(stack, "cst", [1, 8], F32)
    b2_sb = alloc(stack, "b2_sb", [128, 8], F32)
    PS0 = stack.enter_context(nc.psum_tensor("PS0", [128, 2048], F32))
    PS1 = stack.enter_context(nc.psum_tensor("PS1", [128, 2048], F32))

    def pk(tname, lo, hi):
        return [(tname, b) for b in range(lo // 512, (hi + 511) // 512)]

    grpM = ExitStack()
    I_bf = alloc(grpM, "I_bf", [128, 128], BF16, side="right")
    I_f32 = alloc(grpM, "I_f32", [128, 128], F32, side="right")
    M_TU = alloc(grpM, "M_TU", [128, 128], BF16, side="right")
    M_TL = alloc(grpM, "M_TL", [128, 128], BF16, side="right")

    p.op("gpsimd", lambda e: e.memset(I_bf[:, :], 0.0), writes=[("Ibf",)])
    p.op("gpsimd", lambda e: e.affine_select(
        out=I_bf[:, :], in_=I_bf[:, :], compare_op=ALU.not_equal, fill=1.0,
        base=0, pattern=[[-1, 128]], channel_multiplier=1),
        reads=[("Ibf",)], writes=[("Ibf",)])
    p.op("gpsimd", lambda e: e.memset(I_f32[:, :], 0.0), writes=[("If32",)])
    p.op("gpsimd", lambda e: e.affine_select(
        out=I_f32[:, :], in_=I_f32[:, :], compare_op=ALU.not_equal, fill=1.0,
        base=0, pattern=[[-1, 128]], channel_multiplier=1),
        reads=[("If32",)], writes=[("If32",)])
    p.op("gpsimd", lambda e: e.memset(M_TU[:, :], 0.0), writes=[("MTU",)])
    p.op("gpsimd", lambda e: e.affine_select(
        out=M_TU[:, :], in_=M_TU[:, :], compare_op=ALU.is_ge, fill=MASKVAL,
        base=0, pattern=[[1, 128]], channel_multiplier=-1),
        reads=[("MTU",)], writes=[("MTU",)])
    p.op("gpsimd", lambda e: e.memset(M_TL[:, :], 0.0), writes=[("MTL",)])
    p.op("gpsimd", lambda e: e.affine_select(
        out=M_TL[:, :], in_=M_TL[:, :], compare_op=ALU.is_ge, fill=MASKVAL,
        base=0, pattern=[[-1, 128]], channel_multiplier=1),
        reads=[("MTL",)], writes=[("MTL",)])
    p.op("vector", lambda e: e.memset(ones_col[:, :], 1.0), writes=[("onc",)])
    p.op("vector", lambda e: e.memset(cst[0:1, 0:1], 1e-6), writes=[("cst",)])
    p.op("vector", lambda e: e.memset(cst[0:1, 1:2], BCONST), writes=[("cst",)])
    p.op("vector", lambda e: e.memset(cst[0:1, 2:3], EPS), writes=[("cst",)])
    p.op("vector", lambda e: e.memset(ones_row[:, :], 1.0), writes=[("onr",)])
    p.op("vector", lambda e: e.memset(ones_rowf[:, :], 1.0), writes=[("onrf",)])
    p.dma("ld0", "sync", lambda e: e.dma_start(out=b1_sb[:, :], in_=i_b1[:, :]),
          writes=[("b1",)])
    p.dma("ld0", "sync", lambda e: e.dma_start(out=b2_sb[:, :], in_=i_b2[:, :]),
          writes=[("b2",)])

    grpQKV = ExitStack()
    q_pairs = [alloc(grpQKV, f"qp{i}", [128, T], BF16, side="right")
               for i in range(8)]
    k_pairs = [alloc(grpQKV, f"kp{i}", [128, T], BF16, side="right")
               for i in range(8)]
    vT_all = alloc(grpQKV, "vT_all", [128, NTT * 1040], BF16, side="right")
    vt8_sb = alloc(grpQKV, "vt8_sb", [1, 1040], BF16, side="right")
    m_all = alloc(grpQKV, "m_all", [128, 256], F32, side="right")
    mx_sb = alloc(grpQKV, "mx_sb", [128, 8], F32, side="right")

    grpB = ExitStack()
    stgB = [alloc(grpB, f"stgB{i}", [128, 1024], F32, side="right")
            for i in range(4)]
    x_bf = alloc(grpB, "x_bf", [128, NCC * T], BF16, side="right")
    wqk_bf = alloc(grpB, "wqk_bf", [128, NCC * 2048], BF16, side="right")
    wv_bf = alloc(grpB, "wv_bf", [128, NCC * 1024], BF16, side="right")

    pieceno = [0]
    LQ = (("ld0", "sync"), ("ld1", "sync"), ("ld2", "sync"),
          ("ld3", "sync"))

    def load_cast(dst, dstkey, off, hbm_ap, n, stg, streams=None):
        i = pieceno[0]
        pieceno[0] += 1
        k = i % len(stg)
        s = stg[k]
        skey = ("stg", id(stg), k)
        stream, qeng = LQ[k] if streams is None else streams[k]
        p.dma(stream, qeng,
              lambda e: e.dma_start(out=s[:, 0:n], in_=hbm_ap), writes=[skey])
        p.op("scalar",
             lambda e: e.activation(dst[:, off:off + n], s[:, 0:n], AF.Copy),
             reads=[skey], writes=[dstkey])

    for cc in range(NCC):
        for h2 in range(2):
            load_cast(x_bf, ("xbf", cc), cc * T + 1024 * h2,
                      i_x[128 * cc:128 * cc + 128,
                          1024 * h2:1024 * h2 + 1024], 1024, stgB)
            load_cast(wqk_bf, ("wqk", cc), cc * 2048 + 1024 * h2,
                      i_wqk[128 * cc:128 * cc + 128,
                            1024 * h2:1024 * h2 + 1024], 1024, stgB)
    wv3 = i_wv.rearrange("(c p) o -> p c o", p=128)
    for i in range(8):
        load_cast(wv_bf, ("wv", i // 2), i * 1024, wv3[:, i:i + 1, :],
                  1024, stgB)

    PSX = (PS0, PS1)
    PSN = ("P0", "P1")
    strip_q = []

    def pump(n):
        for _ in range(n):
            if strip_q:
                strip_q.pop(0)()

    sp_par = [0]

    def strip_jobs_B(p2):
        jobs = []
        for i in range(NTT):
            kend = 128 * (i + 1)
            dcb = i // 4

            for cb in range(dcb + 1):
                a = 512 * cb
                w = min(a + 512, kend) - a
                diag = (cb == dcb)

                def piece(a=a, w=w, i=i, cb=cb, dcb=dcb, diag=diag, p2=p2):
                    par = sp_par[0] & 1
                    sp_par[0] += 1
                    PS, PSn = PSX[par], PSN[par]
                    for hh in range(2):
                        lo = 1024 + 512 * hh
                        prow = 64 * hh
                        p.op("tensor", lambda e, lo=lo, a=a, w=w, i=i, p2=p2,
                             prow=prow, PS=PS: e.matmul(
                                PS[:, lo:lo + w],
                                q_pairs[p2][prow:prow + 64,
                                            128 * i:128 * i + 128],
                                k_pairs[p2][prow:prow + 64, a:a + w],
                                start=True, stop=True),
                             reads=[("q", p2), ("k", p2)],
                             writes=[(PSn, 2 + hh)])
                        if diag:
                            doff = w - 128
                            p.op("tensor", lambda e, lo=lo, doff=doff, PS=PS:
                                 e.matmul(PS[:, lo + doff:lo + doff + 128],
                                          I_bf[:, :], M_TL[:, :],
                                          start=False, stop=True,
                                          skip_group_check=True),
                                 reads=[("Ibf",), ("MTL",)],
                                 writes=[(PSn, 2 + hh)])
                    for hh in range(2):
                        lo = 1024 + 512 * hh
                        rd = [(PSn, 2 + hh)]
                        if dcb == 0:
                            mcol = 32 * p2 + 16 * hh + i
                            p.op("vector", lambda e, lo=lo, w=w, mcol=mcol,
                                 PS=PS: e.reduce_max(
                                    m_all[:, mcol:mcol + 1],
                                    PS[:, lo:lo + w], axis=AX.X, negate=True),
                                 reads=rd, writes=[("msb", p2)])
                        else:
                            xcol = 4 * hh + cb
                            p.op("vector", lambda e, lo=lo, w=w, xcol=xcol,
                                 PS=PS: e.reduce_max(
                                    mx_sb[:, xcol:xcol + 1],
                                    PS[:, lo:lo + w], axis=AX.X),
                                 reads=rd, writes=[("mxsb",)])
                jobs.append(piece)

            if dcb > 0:
                def comb(i=i, dcb=dcb, p2=p2):
                    for hh in range(2):
                        mcol = 32 * p2 + 16 * hh + i
                        x0 = 4 * hh
                        p.op("vector", lambda e, mcol=mcol, x0=x0, n=dcb + 1:
                             e.reduce_max(m_all[:, mcol:mcol + 1],
                                          mx_sb[:, x0:x0 + n], axis=AX.X,
                                          negate=True),
                             reads=[("mxsb",)], writes=[("msb", p2)])
                jobs.append(comb)
        return jobs

    for sl in range(16):
        pair, is_q = sl // 2, (sl % 2 == 0)
        woff = 128 * pair if is_q else 1024 + 128 * pair
        dstt = q_pairs[pair] if is_q else k_pairs[pair]
        dkey = ("q", pair) if is_q else ("k", pair)
        scl = SC if is_q else 1.0
        for hf in range(2):
            PSn, PS = PSN[hf], PSX[hf]
            for n4 in (0, 1):
                nn = 2 * hf + n4
                for cc in range(NCC):
                    p.op("tensor", lambda e, PS=PS, n4=n4, nn=nn, cc=cc,
                         woff=woff: e.matmul(
                            PS[:, 512 * n4:512 * n4 + 512],
                            wqk_bf[:, cc * 2048 + woff:cc * 2048 + woff + 128],
                            x_bf[:, cc * T + 512 * nn:cc * T + 512 * nn + 512],
                            start=(cc == 0), stop=(cc == NCC - 1)),
                         reads=[("wqk", cc), ("xbf", cc)],
                         writes=[(PSn, n4)])
                pump(3)
            p.op("scalar", lambda e, PS=PS, dstt=dstt, scl=scl, hf=hf:
                 e.activation(dstt[:, 1024 * hf:1024 * hf + 1024],
                              PS[:, 0:1024], AF.Copy, scale=scl),
                 reads=[(PSn, 0), (PSn, 1)], writes=[dkey])
        if not is_q:
            strip_q.extend(strip_jobs_B(pair))

    ov = vT_all[:, :].rearrange("p (t h x) -> p t h x", t=NTT, x=65)[:, :, :, 64:65]
    p.op("vector", lambda e: e.memset(ov, 1.0), writes=[("vTones",)])
    for tt in range(NTT):
        PSn, PS = PSN[tt % 2], PSX[tt % 2]
        for hf in range(2):
            for cc in range(NCC):
                p.op("tensor", lambda e, PS=PS, cc=cc, hf=hf, tt=tt: e.matmul(
                    PS[:, 512 * hf:512 * hf + 512],
                    x_bf[:, cc * T + 128 * tt:cc * T + 128 * tt + 128],
                    wv_bf[:, cc * 1024 + 512 * hf:cc * 1024 + 512 * hf + 512],
                    start=(cc == 0), stop=(cc == NCC - 1)),
                    reads=[("xbf", cc), ("wv", cc // 2)],
                    writes=[(PSn, hf)])
            pump(7)
        vview = vT_all[:, 1040 * tt:1040 * tt + 1040].rearrange(
            "p (h x) -> p h x", x=65)[:, :, 0:64]
        pview = PS[:, 0:1024].rearrange("p (h d) -> p h d", d=64)
        p.op("scalar", lambda e, vview=vview, pview=pview: e.activation(
            vview, pview, AF.Copy),
            reads=[(PSn, 0), (PSn, 1)], writes=[("vT", tt)])
    while strip_q:
        strip_q.pop(0)()
    for tt in range(NTT):
        for a, b in ((0, 512), (512, 1024), (1024, 1040)):
            p.op("tensor", lambda e, tt=tt, a=a, b=b: e.matmul(
                PS1[0:1, a:b], ones_col[:, :],
                vT_all[:, 1040 * tt + a:1040 * tt + b],
                start=(tt == 0), stop=(tt == NTT - 1)),
                reads=[("vT", tt), ("onc",), ("vTones",)],
                writes=[("P1", a // 512)])
    p.op("scalar", lambda e: e.activation(vt8_sb[:, :], PS1[0:1, 0:1040],
                                          AF.Copy, scale=1e-8),
         reads=pk("P1", 0, 1040), writes=[("vt8",)])

    p.barrier()
    grpB.close()

    grpO = ExitStack()
    O_pairs = [alloc(grpO, f"op{i}", [128, T], BF16) for i in range(8)]
    grpC = ExitStack()
    q_stage = [alloc(grpC, f"qs{i}", [65, T], BF16, side="right")
               for i in range(4)]
    k_stage = [alloc(grpC, f"ks{i}", [65, T], BF16, side="right")
               for i in range(2)]
    pt_buf = [alloc(grpC, f"pt{i}", [128, T], BF16, side="right")
              for i in range(4)]
    mt_sb = alloc(grpC, "mt_sb", [32, 128], BF16, side="right")
    lnw = alloc(grpC, "lnw", [1, T], F32, side="right")
    u_bf = alloc(grpC, "u_bf", [1, T], BF16, side="right")
    abf_sb = alloc(grpC, "abf_sb", [1, T], BF16, side="right")
    alo_sb = alloc(grpC, "alo_sb", [1, T], BF16, side="right")
    ba_sb = alloc(grpC, "ba_sb", [64, 512], F32, side="right")

    for i in range(2):
        p.op("vector", lambda e, i=i: e.memset(k_stage[i][64:65, :], 1.0),
             writes=[("ks1", i)])

    def emit_staging(h):
        pair, prow = h // 2, 64 * (h % 2)
        qb4, hb = h % 4, h % 2
        p.dma("sq", "gpsimd", lambda e, pair=pair, prow=prow, qb4=qb4:
              e.dma_start(out=q_stage[qb4][0:64, :],
                          in_=q_pairs[pair][prow:prow + 64, :]),
              reads=[("q", pair)], writes=[("qs0", qb4)])
        p.dma("sk", "gpsimd", lambda e, pair=pair, prow=prow, hb=hb:
              e.dma_start(out=k_stage[hb][0:64, :],
                          in_=k_pairs[pair][prow:prow + 64, :]),
              reads=[("k", pair)], writes=[("ks0", hb)])

    def emit_plumb(p2):
        p.op("tensor", lambda e, p2=p2: e.transpose(
            PS0[0:32, 1024:1152], m_all[:, 32 * p2:32 * p2 + 32], I_f32[:, :]),
             reads=[("msb", p2), ("If32",)], writes=[("P0", 2)])
        p.op("scalar", lambda e: e.activation(mt_sb[:, :], PS0[0:32, 1024:1152],
                                              AF.Copy),
             reads=[("P0", 2)], writes=[("mt",)])
        for hh in range(2):
            qb4 = (2 * p2 + hh) % 4
            p.dma("mr", "sync", lambda e, hh=hh, qb4=qb4: e.dma_start(
                out=q_stage[qb4][64:65, :].rearrange("o (a b) -> o a b", a=16),
                in_=mt_sb[16 * hh:16 * hh + 16, :]),
                reads=[("mt",)], writes=[("qs1", qb4)])

    def emit_st(h, j):
        hb = h % 2
        qb4 = h % 4
        qs_keys = [("qs0", qb4), ("qs1", qb4)]
        ks_keys = [("ks0", hb), ("ks1", hb)]
        q0 = 128 * j
        pt = pt_buf[j % 4]
        ptkey = ("pt", j % 4)
        nch = (T - q0 + 511) // 512
        boff = 2 * (j % 2) if nch <= 2 else 0
        chunks = []
        pos = q0
        ci = 0
        while pos < T:
            b = min(T, pos + 512)
            chunks.append((pos, b - pos, ci + boff))
            pos = b
            ci += 1
        ci = 0
        while ci < len(chunks):
            group = chunks[ci:ci + 2] if ci + 1 < len(chunks) else chunks[ci:ci + 1]
            for (cpos, w, cci) in group:
                lo = 512 * cci
                p.op("tensor", lambda e, lo=lo, w=w, j=j, hb=hb, qb4=qb4,
                     cpos=cpos: e.matmul(
                        PS0[:, lo:lo + w],
                        k_stage[hb][:, 128 * j:128 * j + 128],
                        q_stage[qb4][:, cpos:cpos + w],
                        start=True, stop=True),
                     reads=qs_keys + ks_keys, writes=[("P0", cci)])
                if cpos == q0:
                    p.op("tensor", lambda e, lo=lo: e.matmul(
                        PS0[:, lo:lo + 128], I_bf[:, :], M_TU[:, :],
                        start=False, stop=True, skip_group_check=True),
                        reads=[("Ibf",), ("MTU",)], writes=[("P0", cci)])
            lo0 = 512 * group[0][2]
            span = sum(g[1] for g in group)
            cpos = group[0][0]
            rd = [("P0", g[2]) for g in group]
            p.op("scalar", lambda e, lo0=lo0, span=span, pt=pt, cpos=cpos,
                 q0=q0: e.activation(pt[:, cpos - q0:cpos - q0 + span],
                                     PS0[:, lo0:lo0 + span], AF.Exp),
                 reads=rd, writes=[ptkey])
            ci += len(group)
    def emit_av(h, j):
        q0 = 128 * j
        pt = pt_buf[j % 4]
        ptkey = ("pt", j % 4)
        pos = q0
        while pos < T:
            b = min(T, (pos // 512 + 1) * 512)
            bank = pos // 512
            p.op("tensor", lambda e, j=j, pos=pos, b=b, pt=pt, q0=q0, h=h,
                 last=(j == min(15, 4 * bank + 3)): e.matmul(
                    PS1[0:65, pos:b],
                    vT_all[:, 1040 * j + 65 * h:1040 * j + 65 * h + 65],
                    pt[:, pos - q0:b - q0],
                    start=(j == 0), stop=last),
                 reads=[ptkey, ("vT", j), ("vTones",)],
                 writes=[("P1", bank)])
            pos = b

    def emit_epi_scalar(h, half):
        a = 1024 * half
        qbs = (2 * half, 2 * half + 1)
        rd = [("P1", qb) for qb in qbs]
        wu = [("ubf", qb) for qb in qbs]
        wl = [("lnw", qb) for qb in qbs]
        wa = [("alpha", qb) for qb in qbs]
        wab = [("abf", qb) for qb in qbs]
        wal = [("alo", qb) for qb in qbs]
        p.op("scalar", lambda e, a=a: e.activation(
            u_bf[0:1, a:a + 1024], PS1[64:65, a:a + 1024], AF.Copy, bias=1e-6),
            reads=rd, writes=wu)
        p.op("scalar", lambda e, a=a: e.activation(
            lnw[0:1, a:a + 1024], PS1[64:65, a:a + 1024], AF.Ln, scale=ACONST,
            bias=cst[0:1, 1:2]),
            reads=rd + [("cst",)], writes=wl + wa)
        p.op("scalar", lambda e, a=a: e.activation(
            lnw[0:1, a:a + 1024], lnw[0:1, a:a + 1024], AF.Exp, scale=-1.0),
            reads=wl, writes=wa)
        p.op("vector", lambda e, a=a: e.tensor_copy(
            abf_sb[0:1, a:a + 1024], lnw[0:1, a:a + 1024]),
            reads=wa, writes=wab)
        p.op("vector", lambda e, a=a: e.tensor_sub(
            alo_sb[0:1, a:a + 1024], lnw[0:1, a:a + 1024],
            abf_sb[0:1, a:a + 1024]),
            reads=wa + wab, writes=wal)

    def emit_epi_piece(h, qb):
        pair, prow = h // 2, 64 * (h % 2)
        a = 512 * qb
        p.op("tensor", lambda e, a=a, h=h: e.matmul(
            PS1[0:64, a:a + 512], vt8_sb[0:1, 65 * h:65 * h + 64],
            u_bf[0:1, a:a + 512],
            start=False, stop=True, skip_group_check=True),
            reads=[("vt8",), ("ubf", qb)], writes=[("P1", qb)])
        p.op("tensor", lambda e, a=a: e.matmul(
            PS1[64:128, a:a + 512], ones_row[0:1, 0:64],
            abf_sb[0:1, a:a + 512], start=True, stop=False),
            reads=[("onr",), ("abf", qb)], writes=[("P1", qb)])
        p.op("tensor", lambda e, a=a: e.matmul(
            PS1[64:128, a:a + 512], ones_row[0:1, 0:64],
            alo_sb[0:1, a:a + 512], start=False, stop=True),
            reads=[("onr",), ("alo", qb)], writes=[("P1", qb)])
        p.op("vector", lambda e, a=a: e.tensor_copy(
            ba_sb[:, :], PS1[64:128, a:a + 512]),
            reads=[("P1", qb)], writes=[("ba",)])
        p.op("vector", lambda e, a=a, pair=pair, prow=prow:
             e.tensor_mul(O_pairs[pair][prow:prow + 64, a:a + 512],
                          PS1[0:64, a:a + 512], ba_sb[:, :]),
             reads=[("P1", qb), ("ba",)], writes=[("O", pair)])

    emit_staging(0)
    emit_staging(1)
    emit_plumb(0)
    for h in range(H):
        nxt = h + 1
        if nxt < H:
            emit_staging(nxt)
        for j in range(NTT):
            emit_st(h, j)
            if h > 0 and j == 0:
                emit_epi_piece(h - 1, 3)
            if j >= 3:
                emit_av(h, j - 3)
            if j == 15:
                emit_av(h, 13)
            if j == 11:
                emit_epi_scalar(h, 0)
            elif j == 12:
                emit_epi_piece(h, 0)
            elif j == 13:
                emit_epi_piece(h, 1)
        emit_av(h, NTT - 2)
        emit_av(h, NTT - 1)
        emit_epi_scalar(h, 1)
        emit_epi_piece(h, 2)
        if h % 2 == 1 and h // 2 + 1 < H // 2:
            emit_plumb(h // 2 + 1)
    emit_epi_piece(H - 1, 3)

    grpDw = ExitStack()
    wp_ot = [alloc(grpDw, f"wpot{i}", [128, 1024], BF16) for i in range(2)]
    wpstg = alloc(grpDw, "wpstg", [128, 1024], F32)
    wp3 = i_wp.rearrange("(c p) o -> p c o", p=128)
    p.dma("wp", "sync", lambda e: e.dma_start(
        out=wpstg[:, :].rearrange("p (c o) -> p c o", c=8),
        in_=wp3[:, :, 0:128]),
        writes=[("stgW", 3)])

    p.barrier()
    grpC.close()
    grpQKV.close()
    grpM.close()

    grpD = ExitStack()
    x_f32 = alloc(grpD, "x_f32", [128, NCC * T], F32)
    stgD = [alloc(grpD, f"stgD{i}", [128, 1024], F32) for i in range(2)]
    sm1 = alloc(grpD, "sm1", [1, T], F32)
    sm3 = alloc(grpD, "sm3", [1, T], F32)
    rbf = [alloc(grpD, f"rbf{i}", [128, 1024], BF16) for i in range(2)]
    sqbf = [alloc(grpD, f"sqbf{i}", [128, 1024], BF16) for i in range(2)]
    grpE = ExitStack()
    U1_bf = alloc(grpE, "U1_bf", [128, NCC * RK], BF16, side="right")
    V1_bf = [alloc(grpE, f"V1_bf{i}", [128, DFF], BF16, side="right")
             for i in range(2)]
    U2_bf = alloc(grpE, "U2_bf", [128, 32 * RK], BF16, side="right")
    V2_bf = alloc(grpE, "V2_bf", [128, 2 * D], BF16, side="right")
    P_bf = [alloc(grpE, f"P_bf{i}", [128, 1024], BF16, side="right")
            for i in range(2)]
    H_bf = [alloc(grpE, f"H_bf{i}", [128, 1024], BF16, side="right")
            for i in range(2)]
    Q_bf = [alloc(grpE, f"Q_bf{i}", [128, 1024], BF16, side="right")
            for i in range(2)]

    stgW = [alloc(grpD, f"stgW{i}", [128, 1024], F32) for i in range(3)]
    stgW.append(wpstg)

    def wp_copy(ot):
        s = stgW[(ot - 1) % 4]
        p.op("vector", lambda e, s=s, ot=ot: e.tensor_copy(
            wp_ot[ot % 2][:, :], s[:, 0:1024]),
            reads=[("stgW", (ot - 1) % 4)], writes=[("wp", ot % 2)])

    def wp_dma(ot):
        s = stgW[(ot - 1) % 4]
        p.dma("wp", "sync", lambda e, s=s, ot=ot: e.dma_start(
            out=s[:, 0:1024], in_=wp3[:, :, 128 * ot:128 * ot + 128]),
            writes=[("stgW", (ot - 1) % 4)])

    for ot in range(1, 4):
        wp_dma(ot)

    def x_dma(cc):
        st = "xr0" if cc % 2 == 0 else "xr1"
        p.dma(st, "sync", lambda e, cc=cc: e.dma_start(
            out=x_f32[:, cc * T:cc * T + T],
            in_=i_x[128 * cc:128 * cc + 128, :]),
            writes=[("xf", cc)])

    x_dma(0)
    x_dma(1)
    LF = (("lf0", "gpsimd"), ("lf1", "gpsimd"))
    u13 = i_u1.rearrange("(c p) r -> p c r", p=128)
    for j in range(2):
        load_cast(U1_bf, ("U1",), j * 1024, u13[:, 4 * j:4 * j + 4, :],
                  1024, stgD, LF)
    for rt in range(2):
        for hf in range(4):
            load_cast(V1_bf[rt], ("V1", rt), hf * 1024,
                      i_v1[128 * rt:128 * rt + 128,
                           hf * 1024:hf * 1024 + 1024],
                      1024, stgD, LF)
    u23 = i_u2.rearrange("(f p) r -> p f r", p=128)
    for j in range(8):
        load_cast(U2_bf, ("U2", j // 2), j * 1024, u23[:, 4 * j:4 * j + 4, :],
                  1024, stgD, LF)
    v23 = i_v2.rearrange("(r p) o -> p r o", p=128)
    for j in range(2):
        load_cast(V2_bf, ("V2",), j * 1024, v23[:, j:j + 1, :], 1024,
                  stgD, LF)

    def ln_stats_half(ot, th, first, last=False):
        buf = th
        o0 = ot * T + 1024 * th
        p.op("vector", lambda e, o0=o0, buf=buf: e.tensor_copy(
            rbf[buf][:, :], x_f32[:, o0:o0 + 1024]),
            reads=[("xf", ot)], writes=[("rbf", buf)])
        p.op("scalar", lambda e, o0=o0, buf=buf: e.activation(
            sqbf[buf][:, :], x_f32[:, o0:o0 + 1024], AF.Square),
            reads=[("xf", ot)], writes=[("sqbf", buf)])
        c0 = 1024 * th
        for n2 in range(2):
            p.op("tensor", lambda e, c0=c0, n2=n2, buf=buf, first=first,
                 last=last: e.matmul(
                    PS1[0:1, c0 + 512 * n2:c0 + 512 * n2 + 512], ones_col[:, :],
                    rbf[buf][:, 512 * n2:512 * n2 + 512],
                    start=first, stop=last),
                 reads=[("onc",), ("rbf", buf)],
                 writes=[("P1", c0 // 512 + n2)])
            p.op("tensor", lambda e, c0=c0, n2=n2, buf=buf: e.matmul(
                PS1[32:33, c0 + 512 * n2:c0 + 512 * n2 + 512], ones_col[:, :],
                sqbf[buf][:, 512 * n2:512 * n2 + 512],
                start=False, stop=False, skip_group_check=True),
                reads=[("onc",), ("sqbf", buf)],
                writes=[("P1", c0 // 512 + n2)])

    def ln_math_bcast_apply(xhat_cast):
        p.op("scalar", lambda e: e.activation(sm1[:, :], PS1[0:1, :], AF.Copy,
                                              scale=1.0 / D),
             reads=[("P1", 0), ("P1", 1), ("P1", 2), ("P1", 3)],
             writes=[("sm1",)])
        p.op("scalar", lambda e: e.activation(sm3[:, :], sm1[:, :], AF.Square),
             reads=[("sm1",)], writes=[("sm3",)])
        p.op("vector", lambda e: e.scalar_tensor_tensor(
            sm3[:, :], PS1[32:33, :], 1.0 / D, sm3[:, :], ALU.mult,
            ALU.subtract),
            reads=[("P1", 0), ("P1", 1), ("P1", 2), ("P1", 3), ("sm3",)],
            writes=[("sm3",)])
        p.op("scalar", lambda e: e.activation(sm3[:, :], sm3[:, :], AF.Ln,
                                              bias=cst[0:1, 2:3]),
             reads=[("sm3",), ("cst",)], writes=[("sm3",)])
        p.op("scalar", lambda e: e.activation(sm3[:, :], sm3[:, :], AF.Exp,
                                              scale=-0.5),
             reads=[("sm3",)], writes=[("sm3",)])
        for n4 in range(4):
            p.op("tensor", lambda e, n4=n4: e.matmul(
                PS0[:, 512 * n4:512 * n4 + 512], ones_rowf[:, :],
                sm1[0:1, 512 * n4:512 * n4 + 512], start=True, stop=True),
                reads=[("onrf",), ("sm1",)], writes=[("P0", n4)])
            p.op("tensor", lambda e, n4=n4: e.matmul(
                PS1[:, 512 * n4:512 * n4 + 512], ones_rowf[:, :],
                sm3[0:1, 512 * n4:512 * n4 + 512], start=True, stop=True),
                reads=[("onrf",), ("sm3",)], writes=[("P1", n4)])
        for cc in range(NCC):
            p.op("vector", lambda e, cc=cc: e.tensor_sub(
                x_f32[:, cc * T:cc * T + T], x_f32[:, cc * T:cc * T + T],
                PS0[:, :]),
                reads=[("xf", cc)] + pk("P0", 0, 2048), writes=[("xf", cc)])
            p.op("vector", lambda e, cc=cc: e.tensor_mul(
                x_f32[:, cc * T:cc * T + T], x_f32[:, cc * T:cc * T + T],
                PS1[:, :]),
                reads=[("xf", cc)] + pk("P1", 0, 2048), writes=[("xf", cc)])
            if xhat_cast:
                p.op("scalar", lambda e, cc=cc: e.activation(
                    O_pairs[cc][:, :], x_f32[:, cc * T:cc * T + T], AF.Copy),
                    reads=[("xf", cc)], writes=[("O", cc)])

    p.op("vector", lambda e: e.tensor_copy(wp_ot[0][:, :], wpstg[:, :]),
         reads=[("stgW", 3)], writes=[("wp", 0)])
    prev = None
    for ot in range(NCC):
        if ot + 1 < NCC:
            wp_copy(ot + 1)
        if ot + 4 < NCC:
            wp_dma(ot + 4)
        if ot + 2 < NCC:
            x_dma(ot + 2)
        for th in range(2):
            lo = 1024 * ((2 * ot + th) % 2)
            o0 = ot * T + 1024 * th
            for n2 in range(2):
                for cc in range(NCC):
                    p.op("tensor", lambda e, lo=lo, n2=n2, cc=cc, ot=ot,
                         th=th: e.matmul(
                            PS0[:, lo + 512 * n2:lo + 512 * n2 + 512],
                            wp_ot[ot % 2][:, 128 * cc:128 * cc + 128],
                            O_pairs[cc][:, 1024 * th + 512 * n2:
                                        1024 * th + 512 * n2 + 512],
                            start=(cc == 0), stop=(cc == NCC - 1)),
                         reads=[("wp", ot % 2), ("O", cc)],
                         writes=[("P0", lo // 512 + n2)])
            if prev is not None:
                pot, pth, plo, po0 = prev
                p.op("vector", lambda e, plo=plo, po0=po0, pot=pot:
                     e.tensor_add(
                        x_f32[:, po0:po0 + 1024], x_f32[:, po0:po0 + 1024],
                        PS0[:, plo:plo + 1024]),
                     reads=[("xf", pot)] + pk("P0", plo, plo + 1024),
                     writes=[("xf", pot)])
                ln_stats_half(pot, pth, pot == 0, last=(pot == NCC - 1))
            prev = (ot, th, lo, o0)
    pot, pth, plo, po0 = prev
    p.op("vector", lambda e, plo=plo, po0=po0, pot=pot: e.tensor_add(
        x_f32[:, po0:po0 + 1024], x_f32[:, po0:po0 + 1024],
        PS0[:, plo:plo + 1024]),
        reads=[("xf", pot)] + pk("P0", plo, plo + 1024), writes=[("xf", pot)])
    ln_stats_half(pot, pth, pot == 0, last=(pot == NCC - 1))
    ln_math_bcast_apply(xhat_cast=True)

    for tb in range(2):
        t0 = 1024 * tb
        for cc in range(NCC):
            for rt in range(2):
                lo = 1024 * rt
                for hf in range(2):
                    p.op("tensor", lambda e, lo=lo, hf=hf, cc=cc, rt=rt,
                         t0=t0: e.matmul(
                            PS0[:, lo + 512 * hf:lo + 512 * hf + 512],
                            U1_bf[:, cc * RK + 128 * rt:
                                  cc * RK + 128 * rt + 128],
                            O_pairs[cc][:, t0 + 512 * hf:t0 + 512 * hf + 512],
                            start=(cc == 0), stop=(cc == NCC - 1)),
                         reads=[("U1",), ("O", cc)],
                         writes=[("P0", (lo + 512 * hf) // 512)])
        for rt in range(2):
            lo = 1024 * rt
            p.op("scalar", lambda e, lo=lo, rt=rt: e.activation(
                P_bf[rt][:, :], PS0[:, lo:lo + 1024], AF.Copy),
                reads=pk("P0", lo, lo + 1024), writes=[("Pbf", rt)])

        def emit_mm2(ft):
            lo = 1024 * (ft % 2)
            for rt in range(2):
                for hf in range(2):
                    p.op("tensor", lambda e, lo=lo, rt=rt, hf=hf, ft=ft:
                         e.matmul(
                            PS1[:, lo + 512 * hf:lo + 512 * hf + 512],
                            V1_bf[rt][:, 128 * ft:128 * ft + 128],
                            P_bf[rt][:, 512 * hf:512 * hf + 512],
                            start=(rt == 0), stop=(rt == 1)),
                         reads=[("V1", rt), ("Pbf", rt)],
                         writes=[("P1", (lo + 512 * hf) // 512)])
            p.op("scalar", lambda e, lo=lo, ft=ft: e.activation(
                H_bf[ft % 2][:, :], PS1[:, lo:lo + 1024], AF.Gelu_apprx_tanh,
                bias=b1_sb[:, ft:ft + 1]),
                reads=pk("P1", lo, lo + 1024) + [("b1",)],
                writes=[("Hbf", ft % 2)])

        def emit_mm3(ft):
            for rt in range(2):
                lo2 = 1024 * rt
                for hf in range(2):
                    p.op("tensor", lambda e, lo2=lo2, hf=hf, ft=ft, rt=rt:
                         e.matmul(
                            PS0[:, lo2 + 512 * hf:lo2 + 512 * hf + 512],
                            U2_bf[:, ft * RK + 128 * rt:
                                  ft * RK + 128 * rt + 128],
                            H_bf[ft % 2][:, 512 * hf:512 * hf + 512],
                            start=(ft == 0), stop=(ft == 31)),
                         reads=[("U2", ft // 8), ("Hbf", ft % 2)],
                         writes=[("P0", (lo2 + 512 * hf) // 512)])

        emit_mm2(0)
        for ft in range(1, 32):
            emit_mm2(ft)
            emit_mm3(ft - 1)
        emit_mm3(31)
        for rt in range(2):
            lo = 1024 * rt
            p.op("scalar", lambda e, lo=lo, rt=rt: e.activation(
                Q_bf[rt][:, :], PS0[:, lo:lo + 1024], AF.Copy),
                reads=pk("P0", lo, lo + 1024), writes=[("Qbf", rt)])

        def emit_mm4(ot):
            lo = 1024 * (ot % 2)
            for rt in range(2):
                for hf in range(2):
                    p.op("tensor", lambda e, lo=lo, rt=rt, hf=hf, ot=ot:
                         e.matmul(
                            PS1[:, lo + 512 * hf:lo + 512 * hf + 512],
                            V2_bf[:, 1024 * rt + 128 * ot:
                                  1024 * rt + 128 * ot + 128],
                            Q_bf[rt][:, 512 * hf:512 * hf + 512],
                            start=(rt == 0), stop=(rt == 1)),
                         reads=[("V2",), ("Qbf", rt)],
                         writes=[("P1", (lo + 512 * hf) // 512)])

        def emit_resid(ot):
            lo = 1024 * (ot % 2)
            p.op("vector", lambda e, lo=lo, ot=ot, t0=t0:
                 e.scalar_tensor_tensor(
                    x_f32[:, ot * T + t0:ot * T + t0 + 1024],
                    PS1[:, lo:lo + 1024], b2_sb[:, ot:ot + 1],
                    x_f32[:, ot * T + t0:ot * T + t0 + 1024],
                    ALU.add, ALU.add),
                 reads=pk("P1", lo, lo + 1024) + [("b2",), ("xf", ot)],
                 writes=[("xf", ot)])

        emit_mm4(0)
        for ot in range(1, NCC):
            emit_mm4(ot)
            emit_resid(ot - 1)
        emit_resid(NCC - 1)

    for ot in range(NCC):
        for th in range(2):
            ln_stats_half(ot, th, ot == 0, last=(ot == NCC - 1))
    ln_math_bcast_apply(xhat_cast=False)
    for cc in range(NCC):
        st, qe = ("out0", "sync") if cc % 2 == 0 else ("out1", "gpsimd")
        p.dma(st, qe, lambda e, cc=cc: e.dma_start(
            out=o_out[128 * cc:128 * cc + 128, :],
            in_=x_f32[:, cc * T:cc * T + T]),
            reads=[("xf", cc)])
    p.op("sync", lambda e: e.nop(),
         extra=(("D", "out0", 4), ("D", "out1", 4)))

    p.finalize(stack)
    grpE.close()
    grpD.close()
    grpDw.close()
    grpO.close()
    stack.close()
    nc.finalize()
    return nc


_CACHE = {}


def kernel(L0_attn_x, L0_attn_w_qkv, L0_attn_w_proj, L0_ffn_U1, L0_ffn_V1,
           L0_ffn_b1, L0_ffn_U2, L0_ffn_V2, L0_ffn_b2):
    if "nc" not in _CACHE:
        _CACHE["nc"] = build_nc()
    nc = _CACHE["nc"]
    f = np.float32
    xT = np.ascontiguousarray(np.transpose(np.asarray(L0_attn_x, f), (0, 2, 1)))
    wqkT = np.ascontiguousarray(np.asarray(L0_attn_w_qkv, f)[0:2048].T)
    wvT = np.ascontiguousarray(np.asarray(L0_attn_w_qkv, f)[2048:3072].T)
    wpT = np.ascontiguousarray(np.asarray(L0_attn_w_proj, f).T)
    b1h = np.ascontiguousarray(np.asarray(L0_ffn_b1, f).reshape(32, 128).T)
    b2h = np.ascontiguousarray(np.asarray(L0_ffn_b2, f).reshape(8, 128).T)
    shared = {
        "wqkT": wqkT, "wvT": wvT, "wpT": wpT,
        "U1": np.ascontiguousarray(np.asarray(L0_ffn_U1, f)),
        "V1": np.ascontiguousarray(np.asarray(L0_ffn_V1, f)),
        "U2": np.ascontiguousarray(np.asarray(L0_ffn_U2, f)),
        "V2": np.ascontiguousarray(np.asarray(L0_ffn_V2, f)),
        "b1h": b1h, "b2h": b2h,
    }
    in_maps = [dict(shared, xT=np.ascontiguousarray(xT[c])) for c in range(8)]
    res = run_bass_kernel_spmd(nc, in_maps, core_ids=list(range(8)))
    out = np.stack([np.asarray(r["out"]).T for r in res.results], axis=0)
    return np.ascontiguousarray(out.astype(np.float32))


# version 25
# speedup vs baseline: 1.0131x; 1.0131x over previous
from contextlib import ExitStack

import numpy as np
import ml_dtypes
import concourse.bass as bass
import concourse.mybir as mybir
from concourse.bass_utils import run_bass_kernel_spmd

F32 = mybir.dt.float32
BF16 = mybir.dt.bfloat16
AF = mybir.ActivationFunctionType
ALU = mybir.AluOpType
AX = mybir.AxisListType

T = 2048
D = 1024
H = 16
HD = 64
DFF = 4096
RK = 256
NCC = D // 128
NTT = T // 128
SC = 0.125
EPS = 1e-5
ACONST = 1.0 + 2048e-8
BCONST = 2048e-14
MASKVAL = -1.0e30


class Prog:

    ENG = ("tensor", "vector", "scalar", "gpsimd", "sync")

    def __init__(self, nc):
        self.nc = nc
        self.eng_ops = {e: [] for e in self.ENG}
        self.eng_seq = {e: 0 for e in self.ENG}
        self.last_w = {}
        self.readers = {}
        self.clock = {e: {} for e in self.ENG}
        self.snap = {}
        self.streams = {}
        self.marked = set()
        self._bar = 0

    def _join(self, eng, ev, ck, v):
        clk = self.clock[eng]
        for k2, v2 in self.snap[ev].items():
            if clk.get(k2, -1) < v2:
                clk[k2] = v2
        if clk.get(ck, -1) < v:
            clk[ck] = v

    def _need(self, eng, ev, waits):
        if ev is None:
            return
        if ev[0] == "E":
            _, e2, s2 = ev
            if e2 == "tensor" and eng == "tensor":
                return
            ck, v = ("E", e2), s2
        else:
            _, st, k = ev
            ck, v = ("D", st), k
        if self.clock[eng].get(ck, -1) >= v:
            return
        waits.append(ev)
        if ev[0] == "E":
            self.marked.add(ev)
        self._join(eng, ev, ck, v)

    def _deps(self, eng, reads, writes, extra):
        waits = []
        for r in reads:
            self._need(eng, self.last_w.get(r), waits)
        for w in writes:
            self._need(eng, self.last_w.get(w), waits)
            for ev in self.readers.get(w, ()):
                self._need(eng, ev, waits)
        for ev in extra:
            self._need(eng, ev, waits)
        return waits

    def op(self, eng, fn, reads=(), writes=(), extra=()):
        waits = self._deps(eng, reads, writes, extra)
        seq = self.eng_seq[eng]
        self.eng_seq[eng] = seq + 1
        me = ("E", eng, seq)
        sn = dict(self.clock[eng])
        sn[("E", eng)] = seq
        self.snap[me] = sn
        for r in reads:
            self.readers.setdefault(r, []).append(me)
        for w in writes:
            self.last_w[w] = me
            self.readers[w] = []
        self.eng_ops[eng].append((fn, waits, me, None))
        return me

    def dma(self, stream, qeng, fn, reads=(), writes=()):
        k = self.streams.get(stream, 0) + 1
        self.streams[stream] = k
        extra = [("D", stream, k - 1)] if k >= 2 else []
        waits = self._deps(qeng, reads, writes, extra)
        ev = ("D", stream, k)
        sn = dict(self.clock[qeng])
        sn[("D", stream)] = k
        self.snap[ev] = sn
        for r in reads:
            self.readers.setdefault(r, []).append(ev)
        for w in writes:
            self.last_w[w] = ev
            self.readers[w] = []
        seq = self.eng_seq[qeng]
        self.eng_seq[qeng] = seq + 1
        me = ("E", qeng, seq)
        sn2 = dict(sn)
        sn2[("E", qeng)] = seq
        self.snap[me] = sn2
        self.eng_ops[qeng].append((fn, waits, me, stream))
        return ev

    def barrier(self):
        self._bar += 1
        b = self._bar
        for e in self.ENG:
            if e == "sync":
                continue
            self.op(e, lambda eng: eng.drain(), writes=((f"__bi{b}", e),))
        extra = [("D", s, k) for s, k in self.streams.items()]
        self.op("sync", lambda eng: eng.nop(),
                reads=tuple((f"__bi{b}", e) for e in self.ENG if e != "sync"),
                writes=((f"__bo{b}",),), extra=tuple(extra))
        for e in self.ENG:
            if e == "sync":
                continue
            self.op(e, lambda eng: eng.nop(), reads=((f"__bo{b}",),))

    def finalize(self, stack):
        nc = self.nc
        psem = {e: stack.enter_context(nc.semaphore(f"p_{e}"))
                for e in self.ENG if any(m[1] == e for m in self.marked)}
        ssem = {s: stack.enter_context(nc.semaphore(f"s_{s}"))
                for s in self.streams}
        ccount = {}
        for e in self.ENG:
            cnt = 0
            arr = []
            for (_, _, me, dmastream) in self.eng_ops[e]:
                if dmastream is None and me in self.marked:
                    cnt += 1
                arr.append(cnt)
            ccount[e] = arr

        def replay(ename, engobj):
            for (fn, waits, me, dmastream) in self.eng_ops[ename]:
                for ev in waits:
                    if ev[0] == "E":
                        engobj.wait_ge(psem[ev[1]], ccount[ev[1]][ev[2]])
                    else:
                        engobj.wait_ge(ssem[ev[1]], 16 * ev[2])
                inst = fn(engobj)
                if dmastream is not None:
                    inst.then_inc(ssem[dmastream], 16)
                elif me in self.marked:
                    inst.then_inc(psem[me[1]], 1)

        block = stack.enter_context(nc.Block())

        @block.tensor
        def _(eng):
            replay("tensor", eng)

        @block.vector
        def _(eng):
            replay("vector", eng)

        @block.scalar
        def _(eng):
            replay("scalar", eng)

        @block.gpsimd
        def _(eng):
            replay("gpsimd", eng)

        @block.sync
        def _(eng):
            replay("sync", eng)


def build_nc():
    nc = bass.Bass("TRN2", target_bir_lowering=False, debug=False)
    i_x = nc.dram_tensor("xT", [D, T], F32, kind="ExternalInput").ap()
    i_wqk = nc.dram_tensor("wqkT", [D, 2 * D], F32, kind="ExternalInput").ap()
    i_wv = nc.dram_tensor("wvT", [D, D], F32, kind="ExternalInput").ap()
    i_wp = nc.dram_tensor("wpT", [D, D], F32, kind="ExternalInput").ap()
    i_u1 = nc.dram_tensor("U1", [D, RK], F32, kind="ExternalInput").ap()
    i_v1 = nc.dram_tensor("V1", [RK, DFF], F32, kind="ExternalInput").ap()
    i_u2 = nc.dram_tensor("U2", [DFF, RK], F32, kind="ExternalInput").ap()
    i_v2 = nc.dram_tensor("V2", [RK, D], F32, kind="ExternalInput").ap()
    i_b1 = nc.dram_tensor("b1h", [128, 32], F32, kind="ExternalInput").ap()
    i_b2 = nc.dram_tensor("b2h", [128, 8], F32, kind="ExternalInput").ap()
    o_out = nc.dram_tensor("out", [D, T], F32, kind="ExternalOutput").ap()

    p = Prog(nc)
    stack = ExitStack()

    def alloc(st, name, shape, dt, side="left"):
        return st.enter_context(nc.sbuf_tensor(name, shape, dt, side=side))

    ones_col = alloc(stack, "ones_col", [128, 1], BF16)
    ones_row = alloc(stack, "ones_row", [1, 128], BF16)
    ones_rowf = alloc(stack, "ones_rowf", [1, 128], F32)
    b1_sb = alloc(stack, "b1_sb", [128, 32], F32)
    cst = alloc(stack, "cst", [1, 8], F32)
    b2_sb = alloc(stack, "b2_sb", [128, 8], F32)
    PS0 = stack.enter_context(nc.psum_tensor("PS0", [128, 2048], F32))
    PS1 = stack.enter_context(nc.psum_tensor("PS1", [128, 2048], F32))

    def pk(tname, lo, hi):
        return [(tname, b) for b in range(lo // 512, (hi + 511) // 512)]

    grpM = ExitStack()
    I_bf = alloc(grpM, "I_bf", [128, 128], BF16, side="right")
    I_f32 = alloc(grpM, "I_f32", [128, 128], F32, side="right")
    M_TU = alloc(grpM, "M_TU", [128, 128], BF16, side="right")
    M_TL = alloc(grpM, "M_TL", [128, 128], BF16, side="right")

    p.op("gpsimd", lambda e: e.memset(I_bf[:, :], 0.0), writes=[("Ibf",)])
    p.op("gpsimd", lambda e: e.affine_select(
        out=I_bf[:, :], in_=I_bf[:, :], compare_op=ALU.not_equal, fill=1.0,
        base=0, pattern=[[-1, 128]], channel_multiplier=1),
        reads=[("Ibf",)], writes=[("Ibf",)])
    p.op("gpsimd", lambda e: e.memset(I_f32[:, :], 0.0), writes=[("If32",)])
    p.op("gpsimd", lambda e: e.affine_select(
        out=I_f32[:, :], in_=I_f32[:, :], compare_op=ALU.not_equal, fill=1.0,
        base=0, pattern=[[-1, 128]], channel_multiplier=1),
        reads=[("If32",)], writes=[("If32",)])
    p.op("gpsimd", lambda e: e.memset(M_TU[:, :], 0.0), writes=[("MTU",)])
    p.op("gpsimd", lambda e: e.affine_select(
        out=M_TU[:, :], in_=M_TU[:, :], compare_op=ALU.is_ge, fill=MASKVAL,
        base=0, pattern=[[1, 128]], channel_multiplier=-1),
        reads=[("MTU",)], writes=[("MTU",)])
    p.op("gpsimd", lambda e: e.memset(M_TL[:, :], 0.0), writes=[("MTL",)])
    p.op("gpsimd", lambda e: e.affine_select(
        out=M_TL[:, :], in_=M_TL[:, :], compare_op=ALU.is_ge, fill=MASKVAL,
        base=0, pattern=[[-1, 128]], channel_multiplier=1),
        reads=[("MTL",)], writes=[("MTL",)])
    p.op("vector", lambda e: e.memset(ones_col[:, :], 1.0), writes=[("onc",)])
    p.op("vector", lambda e: e.memset(cst[0:1, 0:1], 1e-6), writes=[("cst",)])
    p.op("vector", lambda e: e.memset(cst[0:1, 1:2], BCONST), writes=[("cst",)])
    p.op("vector", lambda e: e.memset(cst[0:1, 2:3], EPS), writes=[("cst",)])
    p.op("vector", lambda e: e.memset(ones_row[:, :], 1.0), writes=[("onr",)])
    p.op("vector", lambda e: e.memset(ones_rowf[:, :], 1.0), writes=[("onrf",)])
    p.dma("ld0", "sync", lambda e: e.dma_start(out=b1_sb[:, :], in_=i_b1[:, :]),
          writes=[("b1",)])
    p.dma("ld0", "sync", lambda e: e.dma_start(out=b2_sb[:, :], in_=i_b2[:, :]),
          writes=[("b2",)])

    grpQKV = ExitStack()
    q_pairs = [alloc(grpQKV, f"qp{i}", [128, T], BF16, side="right")
               for i in range(8)]
    k_pairs = [alloc(grpQKV, f"kp{i}", [128, T], BF16, side="right")
               for i in range(8)]
    vT_all = alloc(grpQKV, "vT_all", [128, NTT * 1040], BF16, side="right")
    vt8_sb = alloc(grpQKV, "vt8_sb", [1, 1040], BF16, side="right")
    m_all = alloc(grpQKV, "m_all", [128, 256], F32, side="right")
    mx_sb = alloc(grpQKV, "mx_sb", [128, 8], F32, side="right")

    grpB = ExitStack()
    stgB = [alloc(grpB, f"stgB{i}", [128, 1024], F32, side="right")
            for i in range(4)]
    x_bf = alloc(grpB, "x_bf", [128, NCC * T], BF16, side="right")
    wqk_bf = alloc(grpB, "wqk_bf", [128, NCC * 2048], BF16, side="right")
    wv_bf = alloc(grpB, "wv_bf", [128, NCC * 1024], BF16, side="right")

    pieceno = [0]
    LQ = (("ld0", "sync"), ("ld1", "sync"), ("ld2", "sync"),
          ("ld3", "sync"))

    def load_cast(dst, dstkey, off, hbm_ap, n, stg, streams=None):
        i = pieceno[0]
        pieceno[0] += 1
        k = i % len(stg)
        s = stg[k]
        skey = ("stg", id(stg), k)
        stream, qeng = LQ[k] if streams is None else streams[k]
        p.dma(stream, qeng,
              lambda e: e.dma_start(out=s[:, 0:n], in_=hbm_ap), writes=[skey])
        p.op("scalar",
             lambda e: e.activation(dst[:, off:off + n], s[:, 0:n], AF.Copy),
             reads=[skey], writes=[dstkey])

    for cc in range(NCC):
        for h2 in range(2):
            load_cast(x_bf, ("xbf", cc), cc * T + 1024 * h2,
                      i_x[128 * cc:128 * cc + 128,
                          1024 * h2:1024 * h2 + 1024], 1024, stgB)
            load_cast(wqk_bf, ("wqk", cc), cc * 2048 + 1024 * h2,
                      i_wqk[128 * cc:128 * cc + 128,
                            1024 * h2:1024 * h2 + 1024], 1024, stgB)
    wv3 = i_wv.rearrange("(c p) o -> p c o", p=128)
    for i in range(8):
        load_cast(wv_bf, ("wv", i // 2), i * 1024, wv3[:, i:i + 1, :],
                  1024, stgB)

    PSX = (PS0, PS1)
    PSN = ("P0", "P1")
    strip_q = []

    def pump(n):
        for _ in range(n):
            if strip_q:
                strip_q.pop(0)()

    sp_par = [0]

    def strip_jobs_B(p2):
        jobs = []
        for i in range(NTT):
            kend = 128 * (i + 1)
            dcb = i // 4

            for cb in range(dcb + 1):
                a = 512 * cb
                w = min(a + 512, kend) - a
                diag = (cb == dcb)

                def piece(a=a, w=w, i=i, cb=cb, dcb=dcb, diag=diag, p2=p2):
                    par = sp_par[0] & 1
                    sp_par[0] += 1
                    PS, PSn = PSX[par], PSN[par]
                    for hh in range(2):
                        lo = 1024 + 512 * hh
                        prow = 64 * hh
                        p.op("tensor", lambda e, lo=lo, a=a, w=w, i=i, p2=p2,
                             prow=prow, PS=PS: e.matmul(
                                PS[:, lo:lo + w],
                                q_pairs[p2][prow:prow + 64,
                                            128 * i:128 * i + 128],
                                k_pairs[p2][prow:prow + 64, a:a + w],
                                start=True, stop=True),
                             reads=[("q", p2), ("k", p2)],
                             writes=[(PSn, 2 + hh)])
                        if diag:
                            doff = w - 128
                            p.op("tensor", lambda e, lo=lo, doff=doff, PS=PS:
                                 e.matmul(PS[:, lo + doff:lo + doff + 128],
                                          I_bf[:, :], M_TL[:, :],
                                          start=False, stop=True,
                                          skip_group_check=True),
                                 reads=[("Ibf",), ("MTL",)],
                                 writes=[(PSn, 2 + hh)])
                    for hh in range(2):
                        lo = 1024 + 512 * hh
                        rd = [(PSn, 2 + hh)]
                        if dcb == 0:
                            mcol = 32 * p2 + 16 * hh + i
                            p.op("vector", lambda e, lo=lo, w=w, mcol=mcol,
                                 PS=PS: e.reduce_max(
                                    m_all[:, mcol:mcol + 1],
                                    PS[:, lo:lo + w], axis=AX.X, negate=True),
                                 reads=rd, writes=[("msb", p2)])
                        else:
                            xcol = 4 * hh + cb
                            p.op("vector", lambda e, lo=lo, w=w, xcol=xcol,
                                 PS=PS: e.reduce_max(
                                    mx_sb[:, xcol:xcol + 1],
                                    PS[:, lo:lo + w], axis=AX.X),
                                 reads=rd, writes=[("mxsb",)])
                jobs.append(piece)

            if dcb > 0:
                def comb(i=i, dcb=dcb, p2=p2):
                    for hh in range(2):
                        mcol = 32 * p2 + 16 * hh + i
                        x0 = 4 * hh
                        p.op("vector", lambda e, mcol=mcol, x0=x0, n=dcb + 1:
                             e.reduce_max(m_all[:, mcol:mcol + 1],
                                          mx_sb[:, x0:x0 + n], axis=AX.X,
                                          negate=True),
                             reads=[("mxsb",)], writes=[("msb", p2)])
                jobs.append(comb)
        return jobs

    for sl in range(16):
        pair, is_q = sl // 2, (sl % 2 == 0)
        woff = 128 * pair if is_q else 1024 + 128 * pair
        dstt = q_pairs[pair] if is_q else k_pairs[pair]
        dkey = ("q", pair) if is_q else ("k", pair)
        scl = SC if is_q else 1.0
        for hf in range(2):
            PSn, PS = PSN[hf], PSX[hf]
            for n4 in (0, 1):
                nn = 2 * hf + n4
                for cc in range(NCC):
                    p.op("tensor", lambda e, PS=PS, n4=n4, nn=nn, cc=cc,
                         woff=woff: e.matmul(
                            PS[:, 512 * n4:512 * n4 + 512],
                            wqk_bf[:, cc * 2048 + woff:cc * 2048 + woff + 128],
                            x_bf[:, cc * T + 512 * nn:cc * T + 512 * nn + 512],
                            start=(cc == 0), stop=(cc == NCC - 1)),
                         reads=[("wqk", cc), ("xbf", cc)],
                         writes=[(PSn, n4)])
                pump(3)
            p.op("scalar", lambda e, PS=PS, dstt=dstt, scl=scl, hf=hf:
                 e.activation(dstt[:, 1024 * hf:1024 * hf + 1024],
                              PS[:, 0:1024], AF.Copy, scale=scl),
                 reads=[(PSn, 0), (PSn, 1)], writes=[dkey])
        if not is_q:
            strip_q.extend(strip_jobs_B(pair))

    ov = vT_all[:, :].rearrange("p (t h x) -> p t h x", t=NTT, x=65)[:, :, :, 64:65]
    p.op("vector", lambda e: e.memset(ov, 1.0), writes=[("vTones",)])
    for tt in range(NTT):
        PSn, PS = PSN[tt % 2], PSX[tt % 2]
        for hf in range(2):
            for cc in range(NCC):
                p.op("tensor", lambda e, PS=PS, cc=cc, hf=hf, tt=tt: e.matmul(
                    PS[:, 512 * hf:512 * hf + 512],
                    x_bf[:, cc * T + 128 * tt:cc * T + 128 * tt + 128],
                    wv_bf[:, cc * 1024 + 512 * hf:cc * 1024 + 512 * hf + 512],
                    start=(cc == 0), stop=(cc == NCC - 1)),
                    reads=[("xbf", cc), ("wv", cc // 2)],
                    writes=[(PSn, hf)])
            pump(7)
        vview = vT_all[:, 1040 * tt:1040 * tt + 1040].rearrange(
            "p (h x) -> p h x", x=65)[:, :, 0:64]
        pview = PS[:, 0:1024].rearrange("p (h d) -> p h d", d=64)
        p.op("scalar", lambda e, vview=vview, pview=pview: e.activation(
            vview, pview, AF.Copy),
            reads=[(PSn, 0), (PSn, 1)], writes=[("vT", tt)])
    while strip_q:
        strip_q.pop(0)()
    for tt in range(NTT):
        for a, b in ((0, 512), (512, 1024), (1024, 1040)):
            p.op("tensor", lambda e, tt=tt, a=a, b=b: e.matmul(
                PS1[0:1, a:b], ones_col[:, :],
                vT_all[:, 1040 * tt + a:1040 * tt + b],
                start=(tt == 0), stop=(tt == NTT - 1)),
                reads=[("vT", tt), ("onc",), ("vTones",)],
                writes=[("P1", a // 512)])
    p.op("scalar", lambda e: e.activation(vt8_sb[:, :], PS1[0:1, 0:1040],
                                          AF.Copy, scale=1e-8),
         reads=pk("P1", 0, 1040), writes=[("vt8",)])

    p.barrier()
    grpB.close()

    grpO = ExitStack()
    O_pairs = [alloc(grpO, f"op{i}", [128, T], BF16) for i in range(8)]
    grpC = ExitStack()
    q_stage = [alloc(grpC, f"qs{i}", [65, T], BF16, side="right")
               for i in range(4)]
    k_stage = [alloc(grpC, f"ks{i}", [65, T], BF16, side="right")
               for i in range(2)]
    pt_buf = [alloc(grpC, f"pt{i}", [128, T], BF16, side="right")
              for i in range(4)]
    mt_sb = alloc(grpC, "mt_sb", [32, 128], BF16, side="right")
    lnw = alloc(grpC, "lnw", [1, T], F32, side="right")
    u_bf = alloc(grpC, "u_bf", [1, T], BF16, side="right")
    abf_sb = alloc(grpC, "abf_sb", [1, T], BF16, side="right")
    alo_sb = alloc(grpC, "alo_sb", [1, T], BF16, side="right")
    ba_sb = alloc(grpC, "ba_sb", [64, 512], F32, side="right")

    for i in range(2):
        p.op("vector", lambda e, i=i: e.memset(k_stage[i][64:65, :], 1.0),
             writes=[("ks1", i)])

    def emit_staging(h):
        pair, prow = h // 2, 64 * (h % 2)
        qb4, hb = h % 4, h % 2
        p.dma("sq", "gpsimd", lambda e, pair=pair, prow=prow, qb4=qb4:
              e.dma_start(out=q_stage[qb4][0:64, :],
                          in_=q_pairs[pair][prow:prow + 64, :]),
              reads=[("q", pair)], writes=[("qs0", qb4)])
        p.dma("sk", "gpsimd", lambda e, pair=pair, prow=prow, hb=hb:
              e.dma_start(out=k_stage[hb][0:64, :],
                          in_=k_pairs[pair][prow:prow + 64, :]),
              reads=[("k", pair)], writes=[("ks0", hb)])

    def emit_plumb(p2):
        p.op("tensor", lambda e, p2=p2: e.transpose(
            PS0[0:32, 1024:1152], m_all[:, 32 * p2:32 * p2 + 32], I_f32[:, :]),
             reads=[("msb", p2), ("If32",)], writes=[("P0", 2)])
        p.op("scalar", lambda e: e.activation(mt_sb[:, :], PS0[0:32, 1024:1152],
                                              AF.Copy),
             reads=[("P0", 2)], writes=[("mt",)])
        for hh in range(2):
            qb4 = (2 * p2 + hh) % 4
            p.dma("mr", "sync", lambda e, hh=hh, qb4=qb4: e.dma_start(
                out=q_stage[qb4][64:65, :].rearrange("o (a b) -> o a b", a=16),
                in_=mt_sb[16 * hh:16 * hh + 16, :]),
                reads=[("mt",)], writes=[("qs1", qb4)])

    def emit_st(h, j):
        hb = h % 2
        qb4 = h % 4
        qs_keys = [("qs0", qb4), ("qs1", qb4)]
        ks_keys = [("ks0", hb), ("ks1", hb)]
        q0 = 128 * j
        pt = pt_buf[j % 4]
        ptkey = ("pt", j % 4)
        nch = (T - q0 + 511) // 512
        boff = 2 * (j % 2) if nch <= 2 else 0
        chunks = []
        pos = q0
        ci = 0
        while pos < T:
            b = min(T, pos + 512)
            chunks.append((pos, b - pos, ci + boff))
            pos = b
            ci += 1
        ci = 0
        while ci < len(chunks):
            group = chunks[ci:ci + 2] if ci + 1 < len(chunks) else chunks[ci:ci + 1]
            for (cpos, w, cci) in group:
                lo = 512 * cci
                p.op("tensor", lambda e, lo=lo, w=w, j=j, hb=hb, qb4=qb4,
                     cpos=cpos: e.matmul(
                        PS0[:, lo:lo + w],
                        k_stage[hb][:, 128 * j:128 * j + 128],
                        q_stage[qb4][:, cpos:cpos + w],
                        start=True, stop=True),
                     reads=qs_keys + ks_keys, writes=[("P0", cci)])
                if cpos == q0:
                    p.op("tensor", lambda e, lo=lo: e.matmul(
                        PS0[:, lo:lo + 128], I_bf[:, :], M_TU[:, :],
                        start=False, stop=True, skip_group_check=True),
                        reads=[("Ibf",), ("MTU",)], writes=[("P0", cci)])
            lo0 = 512 * group[0][2]
            span = sum(g[1] for g in group)
            cpos = group[0][0]
            rd = [("P0", g[2]) for g in group]
            p.op("scalar", lambda e, lo0=lo0, span=span, pt=pt, cpos=cpos,
                 q0=q0: e.activation(pt[:, cpos - q0:cpos - q0 + span],
                                     PS0[:, lo0:lo0 + span], AF.Exp),
                 reads=rd, writes=[ptkey])
            ci += len(group)
    def emit_av(h, j):
        q0 = 128 * j
        pt = pt_buf[j % 4]
        ptkey = ("pt", j % 4)
        pos = q0
        while pos < T:
            b = min(T, (pos // 512 + 1) * 512)
            bank = pos // 512
            p.op("tensor", lambda e, j=j, pos=pos, b=b, pt=pt, q0=q0, h=h,
                 last=(j == min(15, 4 * bank + 3)): e.matmul(
                    PS1[0:65, pos:b],
                    vT_all[:, 1040 * j + 65 * h:1040 * j + 65 * h + 65],
                    pt[:, pos - q0:b - q0],
                    start=(j == 0), stop=last),
                 reads=[ptkey, ("vT", j), ("vTones",)],
                 writes=[("P1", bank)])
            pos = b

    def emit_epi_scalar(h, half):
        a = 1024 * half
        qbs = (2 * half, 2 * half + 1)
        rd = [("P1", qb) for qb in qbs]
        wu = [("ubf", qb) for qb in qbs]
        wl = [("lnw", qb) for qb in qbs]
        wa = [("alpha", qb) for qb in qbs]
        wab = [("abf", qb) for qb in qbs]
        wal = [("alo", qb) for qb in qbs]
        p.op("scalar", lambda e, a=a: e.activation(
            u_bf[0:1, a:a + 1024], PS1[64:65, a:a + 1024], AF.Copy, bias=1e-6),
            reads=rd, writes=wu)
        p.op("scalar", lambda e, a=a: e.activation(
            lnw[0:1, a:a + 1024], PS1[64:65, a:a + 1024], AF.Ln, scale=ACONST,
            bias=cst[0:1, 1:2]),
            reads=rd + [("cst",)], writes=wl + wa)
        p.op("scalar", lambda e, a=a: e.activation(
            lnw[0:1, a:a + 1024], lnw[0:1, a:a + 1024], AF.Exp, scale=-1.0),
            reads=wl, writes=wa)
        p.op("vector", lambda e, a=a: e.tensor_copy(
            abf_sb[0:1, a:a + 1024], lnw[0:1, a:a + 1024]),
            reads=wa, writes=wab)
        p.op("vector", lambda e, a=a: e.tensor_sub(
            alo_sb[0:1, a:a + 1024], lnw[0:1, a:a + 1024],
            abf_sb[0:1, a:a + 1024]),
            reads=wa + wab, writes=wal)

    def emit_epi_piece(h, qb):
        pair, prow = h // 2, 64 * (h % 2)
        a = 512 * qb
        p.op("tensor", lambda e, a=a, h=h: e.matmul(
            PS1[0:64, a:a + 512], vt8_sb[0:1, 65 * h:65 * h + 64],
            u_bf[0:1, a:a + 512],
            start=False, stop=True, skip_group_check=True),
            reads=[("vt8",), ("ubf", qb)], writes=[("P1", qb)])
        p.op("tensor", lambda e, a=a: e.matmul(
            PS1[64:128, a:a + 512], ones_row[0:1, 0:64],
            abf_sb[0:1, a:a + 512], start=True, stop=False),
            reads=[("onr",), ("abf", qb)], writes=[("P1", qb)])
        p.op("tensor", lambda e, a=a: e.matmul(
            PS1[64:128, a:a + 512], ones_row[0:1, 0:64],
            alo_sb[0:1, a:a + 512], start=False, stop=True),
            reads=[("onr",), ("alo", qb)], writes=[("P1", qb)])
        p.op("vector", lambda e, a=a: e.tensor_copy(
            ba_sb[:, :], PS1[64:128, a:a + 512]),
            reads=[("P1", qb)], writes=[("ba",)])
        p.op("vector", lambda e, a=a, pair=pair, prow=prow:
             e.tensor_mul(O_pairs[pair][prow:prow + 64, a:a + 512],
                          PS1[0:64, a:a + 512], ba_sb[:, :]),
             reads=[("P1", qb), ("ba",)], writes=[("O", pair)])

    emit_staging(0)
    emit_staging(1)
    emit_plumb(0)
    for h in range(H):
        nxt = h + 1
        if nxt < H:
            emit_staging(nxt)
        for j in range(NTT):
            emit_st(h, j)
            if h > 0 and j == 0:
                emit_epi_piece(h - 1, 3)
            if j >= 3:
                emit_av(h, j - 3)
            if j == 15:
                emit_av(h, 13)
            if j == 11:
                emit_epi_scalar(h, 0)
            elif j == 12:
                emit_epi_piece(h, 0)
            elif j == 13:
                emit_epi_piece(h, 1)
        emit_av(h, NTT - 2)
        emit_av(h, NTT - 1)
        emit_epi_scalar(h, 1)
        emit_epi_piece(h, 2)
        if h % 2 == 1 and h // 2 + 1 < H // 2:
            emit_plumb(h // 2 + 1)
    emit_epi_piece(H - 1, 3)

    grpDw = ExitStack()
    wp_ot = [alloc(grpDw, f"wpot{i}", [128, 1024], BF16) for i in range(2)]
    wpstg = alloc(grpDw, "wpstg", [128, 1024], F32)
    wp3 = i_wp.rearrange("(c p) o -> p c o", p=128)
    p.dma("wp", "sync", lambda e: e.dma_start(
        out=wpstg[:, :].rearrange("p (c o) -> p c o", c=8),
        in_=wp3[:, :, 0:128]),
        writes=[("stgW", 3)])

    p.barrier()
    grpC.close()
    grpQKV.close()
    grpM.close()

    grpD = ExitStack()
    x_f32 = alloc(grpD, "x_f32", [128, NCC * T], F32)
    stgD = [alloc(grpD, f"stgD{i}", [128, 1024], F32) for i in range(2)]
    sm1 = alloc(grpD, "sm1", [1, T], F32)
    sm3 = alloc(grpD, "sm3", [1, T], F32)
    rbf = [alloc(grpD, f"rbf{i}", [128, 1024], BF16) for i in range(2)]
    sqbf = [alloc(grpD, f"sqbf{i}", [128, 1024], BF16) for i in range(2)]
    grpE = ExitStack()
    U1_bf = alloc(grpE, "U1_bf", [128, NCC * RK], BF16, side="right")
    V1_bf = [alloc(grpE, f"V1_bf{i}", [128, DFF], BF16, side="right")
             for i in range(2)]
    U2_bf = alloc(grpE, "U2_bf", [128, 32 * RK], BF16, side="right")
    V2_bf = alloc(grpE, "V2_bf", [128, 2 * D], BF16, side="right")
    P_bf = [alloc(grpE, f"P_bf{i}", [128, 1024], BF16, side="right")
            for i in range(2)]
    H_bf = [alloc(grpE, f"H_bf{i}", [128, 1024], BF16, side="right")
            for i in range(2)]
    Q_bf = [alloc(grpE, f"Q_bf{i}", [128, 1024], BF16, side="right")
            for i in range(2)]

    stgW = [alloc(grpD, f"stgW{i}", [128, 1024], F32) for i in range(3)]
    stgW.append(wpstg)

    def wp_copy(ot):
        s = stgW[(ot - 1) % 4]
        p.op("vector", lambda e, s=s, ot=ot: e.tensor_copy(
            wp_ot[ot % 2][:, :], s[:, 0:1024]),
            reads=[("stgW", (ot - 1) % 4)], writes=[("wp", ot % 2)])

    def wp_dma(ot):
        s = stgW[(ot - 1) % 4]
        p.dma("wp", "sync", lambda e, s=s, ot=ot: e.dma_start(
            out=s[:, 0:1024], in_=wp3[:, :, 128 * ot:128 * ot + 128]),
            writes=[("stgW", (ot - 1) % 4)])

    for ot in range(1, 4):
        wp_dma(ot)

    def x_dma(cc):
        st = "xr0" if cc % 2 == 0 else "xr1"
        p.dma(st, "sync", lambda e, cc=cc: e.dma_start(
            out=x_f32[:, cc * T:cc * T + T],
            in_=i_x[128 * cc:128 * cc + 128, :]),
            writes=[("xf", cc)])

    x_dma(0)
    x_dma(1)
    def emit_lf_loads():
        LF = (("lf0", "sync"), ("lf1", "sync"))
        u13 = i_u1.rearrange("(c p) r -> p c r", p=128)
        for j in range(2):
            load_cast(U1_bf, ("U1",), j * 1024, u13[:, 4 * j:4 * j + 4, :],
                      1024, stgD, LF)
        for rt in range(2):
            for hf in range(4):
                load_cast(V1_bf[rt], ("V1", rt), hf * 1024,
                          i_v1[128 * rt:128 * rt + 128,
                               hf * 1024:hf * 1024 + 1024],
                          1024, stgD, LF)
        u23 = i_u2.rearrange("(f p) r -> p f r", p=128)
        for j in range(8):
            load_cast(U2_bf, ("U2", j // 2), j * 1024,
                      u23[:, 4 * j:4 * j + 4, :], 1024, stgD, LF)
        v23 = i_v2.rearrange("(r p) o -> p r o", p=128)
        for j in range(2):
            load_cast(V2_bf, ("V2",), j * 1024, v23[:, j:j + 1, :], 1024,
                      stgD, LF)

    def ln_stats_half(ot, th, first, last=False):
        buf = th
        o0 = ot * T + 1024 * th
        p.op("vector", lambda e, o0=o0, buf=buf: e.tensor_copy(
            rbf[buf][:, :], x_f32[:, o0:o0 + 1024]),
            reads=[("xf", ot)], writes=[("rbf", buf)])
        p.op("scalar", lambda e, o0=o0, buf=buf: e.activation(
            sqbf[buf][:, :], x_f32[:, o0:o0 + 1024], AF.Square),
            reads=[("xf", ot)], writes=[("sqbf", buf)])
        c0 = 1024 * th
        for n2 in range(2):
            p.op("tensor", lambda e, c0=c0, n2=n2, buf=buf, first=first,
                 last=last: e.matmul(
                    PS1[0:1, c0 + 512 * n2:c0 + 512 * n2 + 512], ones_col[:, :],
                    rbf[buf][:, 512 * n2:512 * n2 + 512],
                    start=first, stop=last),
                 reads=[("onc",), ("rbf", buf)],
                 writes=[("P1", c0 // 512 + n2)])
            p.op("tensor", lambda e, c0=c0, n2=n2, buf=buf: e.matmul(
                PS1[32:33, c0 + 512 * n2:c0 + 512 * n2 + 512], ones_col[:, :],
                sqbf[buf][:, 512 * n2:512 * n2 + 512],
                start=False, stop=False, skip_group_check=True),
                reads=[("onc",), ("sqbf", buf)],
                writes=[("P1", c0 // 512 + n2)])

    def ln_math_bcast_apply(xhat_cast):
        p.op("scalar", lambda e: e.activation(sm1[:, :], PS1[0:1, :], AF.Copy,
                                              scale=1.0 / D),
             reads=[("P1", 0), ("P1", 1), ("P1", 2), ("P1", 3)],
             writes=[("sm1",)])
        p.op("scalar", lambda e: e.activation(sm3[:, :], sm1[:, :], AF.Square),
             reads=[("sm1",)], writes=[("sm3",)])
        p.op("vector", lambda e: e.scalar_tensor_tensor(
            sm3[:, :], PS1[32:33, :], 1.0 / D, sm3[:, :], ALU.mult,
            ALU.subtract),
            reads=[("P1", 0), ("P1", 1), ("P1", 2), ("P1", 3), ("sm3",)],
            writes=[("sm3",)])
        p.op("scalar", lambda e: e.activation(sm3[:, :], sm3[:, :], AF.Ln,
                                              bias=cst[0:1, 2:3]),
             reads=[("sm3",), ("cst",)], writes=[("sm3",)])
        p.op("scalar", lambda e: e.activation(sm3[:, :], sm3[:, :], AF.Exp,
                                              scale=-0.5),
             reads=[("sm3",)], writes=[("sm3",)])
        for n4 in range(4):
            p.op("tensor", lambda e, n4=n4: e.matmul(
                PS0[:, 512 * n4:512 * n4 + 512], ones_rowf[:, :],
                sm1[0:1, 512 * n4:512 * n4 + 512], start=True, stop=True),
                reads=[("onrf",), ("sm1",)], writes=[("P0", n4)])
            p.op("tensor", lambda e, n4=n4: e.matmul(
                PS1[:, 512 * n4:512 * n4 + 512], ones_rowf[:, :],
                sm3[0:1, 512 * n4:512 * n4 + 512], start=True, stop=True),
                reads=[("onrf",), ("sm3",)], writes=[("P1", n4)])
        for cc in range(NCC):
            p.op("vector", lambda e, cc=cc: e.tensor_sub(
                x_f32[:, cc * T:cc * T + T], x_f32[:, cc * T:cc * T + T],
                PS0[:, :]),
                reads=[("xf", cc)] + pk("P0", 0, 2048), writes=[("xf", cc)])
            p.op("vector", lambda e, cc=cc: e.tensor_mul(
                x_f32[:, cc * T:cc * T + T], x_f32[:, cc * T:cc * T + T],
                PS1[:, :]),
                reads=[("xf", cc)] + pk("P1", 0, 2048), writes=[("xf", cc)])
            if xhat_cast:
                p.op("scalar", lambda e, cc=cc: e.activation(
                    O_pairs[cc][:, :], x_f32[:, cc * T:cc * T + T], AF.Copy),
                    reads=[("xf", cc)], writes=[("O", cc)])

    p.op("vector", lambda e: e.tensor_copy(wp_ot[0][:, :], wpstg[:, :]),
         reads=[("stgW", 3)], writes=[("wp", 0)])
    prev = None
    for ot in range(NCC):
        if ot + 1 < NCC:
            wp_copy(ot + 1)
        if ot + 4 < NCC:
            wp_dma(ot + 4)
        if ot + 2 < NCC:
            x_dma(ot + 2)
        for th in range(2):
            lo = 1024 * ((2 * ot + th) % 2)
            o0 = ot * T + 1024 * th
            for n2 in range(2):
                for cc in range(NCC):
                    p.op("tensor", lambda e, lo=lo, n2=n2, cc=cc, ot=ot,
                         th=th: e.matmul(
                            PS0[:, lo + 512 * n2:lo + 512 * n2 + 512],
                            wp_ot[ot % 2][:, 128 * cc:128 * cc + 128],
                            O_pairs[cc][:, 1024 * th + 512 * n2:
                                        1024 * th + 512 * n2 + 512],
                            start=(cc == 0), stop=(cc == NCC - 1)),
                         reads=[("wp", ot % 2), ("O", cc)],
                         writes=[("P0", lo // 512 + n2)])
            if prev is not None:
                pot, pth, plo, po0 = prev
                p.op("vector", lambda e, plo=plo, po0=po0, pot=pot:
                     e.tensor_add(
                        x_f32[:, po0:po0 + 1024], x_f32[:, po0:po0 + 1024],
                        PS0[:, plo:plo + 1024]),
                     reads=[("xf", pot)] + pk("P0", plo, plo + 1024),
                     writes=[("xf", pot)])
                ln_stats_half(pot, pth, pot == 0, last=(pot == NCC - 1))
            prev = (ot, th, lo, o0)
    pot, pth, plo, po0 = prev
    p.op("vector", lambda e, plo=plo, po0=po0, pot=pot: e.tensor_add(
        x_f32[:, po0:po0 + 1024], x_f32[:, po0:po0 + 1024],
        PS0[:, plo:plo + 1024]),
        reads=[("xf", pot)] + pk("P0", plo, plo + 1024), writes=[("xf", pot)])
    ln_stats_half(pot, pth, pot == 0, last=(pot == NCC - 1))
    emit_lf_loads()
    ln_math_bcast_apply(xhat_cast=True)

    for tb in range(2):
        t0 = 1024 * tb
        for cc in range(NCC):
            for rt in range(2):
                lo = 1024 * rt
                for hf in range(2):
                    p.op("tensor", lambda e, lo=lo, hf=hf, cc=cc, rt=rt,
                         t0=t0: e.matmul(
                            PS0[:, lo + 512 * hf:lo + 512 * hf + 512],
                            U1_bf[:, cc * RK + 128 * rt:
                                  cc * RK + 128 * rt + 128],
                            O_pairs[cc][:, t0 + 512 * hf:t0 + 512 * hf + 512],
                            start=(cc == 0), stop=(cc == NCC - 1)),
                         reads=[("U1",), ("O", cc)],
                         writes=[("P0", (lo + 512 * hf) // 512)])
        for rt in range(2):
            lo = 1024 * rt
            p.op("scalar", lambda e, lo=lo, rt=rt: e.activation(
                P_bf[rt][:, :], PS0[:, lo:lo + 1024], AF.Copy),
                reads=pk("P0", lo, lo + 1024), writes=[("Pbf", rt)])

        def emit_mm2(ft):
            lo = 1024 * (ft % 2)
            for rt in range(2):
                for hf in range(2):
                    p.op("tensor", lambda e, lo=lo, rt=rt, hf=hf, ft=ft:
                         e.matmul(
                            PS1[:, lo + 512 * hf:lo + 512 * hf + 512],
                            V1_bf[rt][:, 128 * ft:128 * ft + 128],
                            P_bf[rt][:, 512 * hf:512 * hf + 512],
                            start=(rt == 0), stop=(rt == 1)),
                         reads=[("V1", rt), ("Pbf", rt)],
                         writes=[("P1", (lo + 512 * hf) // 512)])
            p.op("scalar", lambda e, lo=lo, ft=ft: e.activation(
                H_bf[ft % 2][:, :], PS1[:, lo:lo + 1024], AF.Gelu_apprx_tanh,
                bias=b1_sb[:, ft:ft + 1]),
                reads=pk("P1", lo, lo + 1024) + [("b1",)],
                writes=[("Hbf", ft % 2)])

        def emit_mm3(ft):
            for rt in range(2):
                lo2 = 1024 * rt
                for hf in range(2):
                    p.op("tensor", lambda e, lo2=lo2, hf=hf, ft=ft, rt=rt:
                         e.matmul(
                            PS0[:, lo2 + 512 * hf:lo2 + 512 * hf + 512],
                            U2_bf[:, ft * RK + 128 * rt:
                                  ft * RK + 128 * rt + 128],
                            H_bf[ft % 2][:, 512 * hf:512 * hf + 512],
                            start=(ft == 0), stop=(ft == 31)),
                         reads=[("U2", ft // 8), ("Hbf", ft % 2)],
                         writes=[("P0", (lo2 + 512 * hf) // 512)])

        emit_mm2(0)
        for ft in range(1, 32):
            emit_mm2(ft)
            emit_mm3(ft - 1)
        emit_mm3(31)
        for rt in range(2):
            lo = 1024 * rt
            p.op("scalar", lambda e, lo=lo, rt=rt: e.activation(
                Q_bf[rt][:, :], PS0[:, lo:lo + 1024], AF.Copy),
                reads=pk("P0", lo, lo + 1024), writes=[("Qbf", rt)])

        def emit_mm4(ot):
            lo = 1024 * (ot % 2)
            for rt in range(2):
                for hf in range(2):
                    p.op("tensor", lambda e, lo=lo, rt=rt, hf=hf, ot=ot:
                         e.matmul(
                            PS1[:, lo + 512 * hf:lo + 512 * hf + 512],
                            V2_bf[:, 1024 * rt + 128 * ot:
                                  1024 * rt + 128 * ot + 128],
                            Q_bf[rt][:, 512 * hf:512 * hf + 512],
                            start=(rt == 0), stop=(rt == 1)),
                         reads=[("V2",), ("Qbf", rt)],
                         writes=[("P1", (lo + 512 * hf) // 512)])

        def emit_resid(ot):
            lo = 1024 * (ot % 2)
            p.op("vector", lambda e, lo=lo, ot=ot, t0=t0:
                 e.scalar_tensor_tensor(
                    x_f32[:, ot * T + t0:ot * T + t0 + 1024],
                    PS1[:, lo:lo + 1024], b2_sb[:, ot:ot + 1],
                    x_f32[:, ot * T + t0:ot * T + t0 + 1024],
                    ALU.add, ALU.add),
                 reads=pk("P1", lo, lo + 1024) + [("b2",), ("xf", ot)],
                 writes=[("xf", ot)])

        emit_mm4(0)
        for ot in range(1, NCC):
            emit_mm4(ot)
            emit_resid(ot - 1)
        emit_resid(NCC - 1)

    for ot in range(NCC):
        for th in range(2):
            ln_stats_half(ot, th, ot == 0, last=(ot == NCC - 1))
    ln_math_bcast_apply(xhat_cast=False)
    for cc in range(NCC):
        st, qe = ("out0", "sync") if cc % 2 == 0 else ("out1", "gpsimd")
        p.dma(st, qe, lambda e, cc=cc: e.dma_start(
            out=o_out[128 * cc:128 * cc + 128, :],
            in_=x_f32[:, cc * T:cc * T + T]),
            reads=[("xf", cc)])
    p.op("sync", lambda e: e.nop(),
         extra=(("D", "out0", 4), ("D", "out1", 4)))

    p.finalize(stack)
    grpE.close()
    grpD.close()
    grpDw.close()
    grpO.close()
    stack.close()
    nc.finalize()
    return nc


_CACHE = {}


def kernel(L0_attn_x, L0_attn_w_qkv, L0_attn_w_proj, L0_ffn_U1, L0_ffn_V1,
           L0_ffn_b1, L0_ffn_U2, L0_ffn_V2, L0_ffn_b2):
    if "nc" not in _CACHE:
        _CACHE["nc"] = build_nc()
    nc = _CACHE["nc"]
    f = np.float32
    xT = np.ascontiguousarray(np.transpose(np.asarray(L0_attn_x, f), (0, 2, 1)))
    wqkT = np.ascontiguousarray(np.asarray(L0_attn_w_qkv, f)[0:2048].T)
    wvT = np.ascontiguousarray(np.asarray(L0_attn_w_qkv, f)[2048:3072].T)
    wpT = np.ascontiguousarray(np.asarray(L0_attn_w_proj, f).T)
    b1h = np.ascontiguousarray(np.asarray(L0_ffn_b1, f).reshape(32, 128).T)
    b2h = np.ascontiguousarray(np.asarray(L0_ffn_b2, f).reshape(8, 128).T)
    shared = {
        "wqkT": wqkT, "wvT": wvT, "wpT": wpT,
        "U1": np.ascontiguousarray(np.asarray(L0_ffn_U1, f)),
        "V1": np.ascontiguousarray(np.asarray(L0_ffn_V1, f)),
        "U2": np.ascontiguousarray(np.asarray(L0_ffn_U2, f)),
        "V2": np.ascontiguousarray(np.asarray(L0_ffn_V2, f)),
        "b1h": b1h, "b2h": b2h,
    }
    in_maps = [dict(shared, xT=np.ascontiguousarray(xT[c])) for c in range(8)]
    res = run_bass_kernel_spmd(nc, in_maps, core_ids=list(range(8)))
    out = np.stack([np.asarray(r["out"]).T for r in res.results], axis=0)
    return np.ascontiguousarray(out.astype(np.float32))
